# version 04
# baseline (speedup 1.0000x reference)
import contextlib

import numpy as np
import ml_dtypes

import concourse.bass as bass
import concourse.mybir as mybir
from concourse.bass_utils import run_bass_kernel_spmd

F32 = mybir.dt.float32
BF16 = mybir.dt.bfloat16
FP8 = mybir.dt.float8e4
AF = mybir.ActivationFunctionType
ALU = mybir.AluOpType
AX = mybir.AxisListType

MAGIC = 12582912.0
LN2 = float(np.log(2.0))
T_TILES = 16
E = 2048
I_DIM = 8192
NH = 16
HD = 128
ECH = 16
ICH = 64
ISL = 16
HSL = 4
QKV_SL = 12
NCHUNK = 8

_DEBUG = False
_PHASES = ("f1", "at", "f3")


class Sem:
    def __init__(self, handle, name):
        self.h = handle
        self.name = name
        self.n = 0


class Prog:
    ENGINES = ("sy", "sc", "ve", "po", "pe")

    def __init__(self, nc, st):
        self.nc = nc
        self.st = st
        self.cmds = {e: [] for e in self.ENGINES}
        self.sems = {}

    def sem(self, name):
        if name not in self.sems:
            self.sems[name] = Sem(self.st.enter_context(self.nc.semaphore(name)), name)
        return self.sems[name]

    def wait(self, eng, sem, val=None):
        v = sem.n if val is None else val
        if v > 0:
            self.cmds[eng].append(lambda EG, s=sem.h, v=v: EG.wait_ge(s, v))

    def dma(self, eng, out, in_, sem, transpose=False):
        sem.n += 16
        def f(EG, out=out, in_=in_, s=sem.h, tr=transpose):
            if tr:
                EG.dma_start(out=out, in_=in_, transpose=True).then_inc(s, 16)
            else:
                EG.dma_start(out=out, in_=in_).then_inc(s, 16)
        self.cmds[eng].append(f)

    def op(self, eng, fn, sem=None, inc=1):
        if sem is not None:
            sem.n += inc
            def g(EG, fn=fn, s=sem.h, k=inc):
                fn(EG).then_inc(s, k)
            self.cmds[eng].append(g)
        else:
            self.cmds[eng].append(lambda EG, fn=fn: fn(EG))

    def tick(self, eng, sem):
        sem.n += 1
        self.cmds[eng].append(lambda EG, s=sem.h: EG.sem_inc(s, 1))

    def finalize(self, block):
        for eng, attr in (("sy", "sync"), ("sc", "scalar"), ("ve", "vector"),
                          ("po", "gpsimd"), ("pe", "tensor")):
            cmds = self.cmds[eng]
            def run(EG, cmds=cmds):
                for c in cmds:
                    c(EG)
            getattr(block, attr)(run)


def host_tables(half):
    inv_freq = 1.0 / (10000.0 ** (np.arange(0, HD, 2, dtype=np.float32) / HD))
    pos = np.arange(4096, dtype=np.float32)
    freqs = np.outer(pos, inv_freq)
    emb = np.concatenate([freqs, freqs], axis=-1)
    cos = np.cos(emb)[half * 2048:(half + 1) * 2048]
    sin = np.sin(emb)[half * 2048:(half + 1) * 2048]
    sm = sin.copy()
    sm[:, 0::2] = -sm[:, 0::2]
    return (np.asarray(cos, np.float32).astype(ml_dtypes.bfloat16),
            np.asarray(sm, np.float32).astype(ml_dtypes.bfloat16))


def host_tril():
    t0 = (np.arange(256)[None, :] >= np.arange(128)[:, None])
    t1 = (np.arange(256)[None, :] >= (128 + np.arange(128))[:, None])
    return np.concatenate([t0, t1], axis=1).astype(ml_dtypes.bfloat16)


def build_program():
    nc = bass.Bass("TRN2", target_bir_lowering=False,
                   detect_race_conditions=False)
    st = contextlib.ExitStack()

    x_in = nc.dram_tensor("x", [2048, 2048], F32, kind="ExternalInput").ap()
    wg_in = nc.dram_tensor("w_gate", [I_DIM, E], F32, kind="ExternalInput").ap()
    wu_in = nc.dram_tensor("w_up", [I_DIM, E], F32, kind="ExternalInput").ap()
    wd_in = nc.dram_tensor("w_down", [E, I_DIM], F32, kind="ExternalInput").ap()
    wq_in = nc.dram_tensor("w_qkv", [3 * E, E], F32, kind="ExternalInput").ap()
    wo_in = nc.dram_tensor("w_out", [E, E], F32, kind="ExternalInput").ap()
    cos_in = nc.dram_tensor("cos_t", [2048, HD], BF16, kind="ExternalInput").ap()
    sin_in = nc.dram_tensor("sin_t", [2048, HD], BF16, kind="ExternalInput").ap()
    tril_in = nc.dram_tensor("tril_t", [128, 512], BF16, kind="ExternalInput").ap()
    ones_in = nc.dram_tensor("ones_t", [128, 1], BF16, kind="ExternalInput").ap()
    flag_in = nc.dram_tensor("flag_t", [128, 12], F32, kind="ExternalInput").ap()
    y_out = nc.dram_tensor("y", [2048, 2048], F32, kind="ExternalOutput").ap()

    dbg_kind = "ExternalOutput" if _DEBUG else "Internal"
    wgT = nc.dram_tensor("wgT", [ISL, 128, 8192], FP8, kind="Internal").ap()
    wuT = nc.dram_tensor("wuT", [ISL, 128, 8192], FP8, kind="Internal").ap()
    wdT = nc.dram_tensor("wdT", [HSL, 128, 4 * 8192], FP8, kind="Internal").ap()
    wqT = nc.dram_tensor("wqT", [QKV_SL, 128, 8192], BF16, kind=dbg_kind).ap()
    woT = nc.dram_tensor("woT", [HSL, 128, 8192], BF16, kind=dbg_kind).ap()
    interD = nc.dram_tensor("interD", [T_TILES, 128, 8192], BF16, kind=dbg_kind).ap()
    iqTD = nc.dram_tensor("iqTD", [T_TILES, 128, 8192], FP8, kind=dbg_kind).ap()
    qD = nc.dram_tensor("qD", [2048, 2048], BF16, kind=dbg_kind).ap()
    kD = nc.dram_tensor("kD", [2048, 2048], BF16, kind=dbg_kind).ap()
    vD = nc.dram_tensor("vD", [2048, 2048], BF16, kind=dbg_kind).ap()
    ccS = nc.dram_tensor("ccS", [128, NH * 129], F32, kind="Internal").ap()
    ccD = nc.dram_tensor("ccD", [128, NH * 129], F32, kind="Internal").ap()
    ccE = nc.dram_tensor("ccE", [128, NH * 129], F32, kind=dbg_kind).ap()
    dbgX = nc.dram_tensor("dbgX", [128, 32768], FP8, kind=dbg_kind).ap()
    dbgS = nc.dram_tensor("dbgS", [128, 96], F32, kind=dbg_kind).ap()
    dbgA = nc.dram_tensor("dbgA", [128, 256], F32, kind=dbg_kind).ap()
    dbgF = nc.dram_tensor("dbgF", [128, 512], BF16, kind=dbg_kind).ap()
    dbgQ = nc.dram_tensor("dbgQ", [128, 2048], BF16, kind=dbg_kind).ap()
    dbgN = nc.dram_tensor("dbgN", [128, 2048], BF16, kind=dbg_kind).ap()

    def sb(name, shape, dt):
        return st.enter_context(nc.sbuf_tensor(name, shape, dt))

    xres = sb("xres", [128, T_TILES * E], BF16)
    cosT = sb("cosT", [128, T_TILES * HD], BF16)
    sinT = sb("sinT", [128, T_TILES * HD], BF16)
    tril = sb("tril", [128, 512], BF16)
    onesb = sb("onesb", [128, 1], BF16)
    flag = sb("flag", [128, 12], F32)
    Sfp = sb("Sfp", [128, NH * 129], F32)
    stats = sb("stats", [128, 96], F32)
    bigA = sb("bigA", [128, 2048], BF16)
    bigB = sb("bigB", [128, 2048], BF16)
    bigC = sb("bigC", [128, 8192], BF16)
    tf32 = sb("tf32", [128, 1024], F32)
    sa0 = sb("sa0", [128, 256], F32)
    sa1 = sb("sa1", [128, 256], F32)
    sa2 = sb("sa2", [128, 256], F32)
    sa1b = sb("sa1b", [128, 256], BF16)
    sa2b = sb("sa2b", [128, 256], BF16)
    w8ring = sb("w8ring", [128, 4 * 8192], FP8)
    xnT = sb("xnT", [128, T_TILES * E], FP8)
    psum = [st.enter_context(nc.psum_tensor(f"ps{i}", [128, 512], F32))
            for i in range(8)]

    wbf = w8ring[:].bitcast(BF16)
    xv = xnT[:].bitcast(BF16)

    p = Prog(nc, st)
    sIN = p.sem("sIN")
    sTR = p.sem("sTR")
    sWQ = p.sem("sWQ")
    sST = p.sem("sST")
    sV = p.sem("sV")
    sA = p.sem("sA")
    sCC = p.sem("sCC")

    def barrier():
        b = p.sem("bar")
        for eng in Prog.ENGINES:
            if eng == "sy":
                p.wait(eng, sIN)
                p.wait(eng, sTR)
            if eng == "po":
                p.wait(eng, sWQ)
                p.wait(eng, sST)
            p.tick(eng, b)
        for eng in Prog.ENGINES:
            p.wait(eng, b)

    QSRC = lambda: bigC[:, 0:2048]
    QT1 = lambda: bigC[:, 2048:4096]
    QGE = lambda: bigC[:, 4096:6144]
    QY = lambda: bigC[:, 6144:8192]

    def emit_quant(src, dst, mfrac):
        nb = 64
        clip = (7.5 if mfrac == 3 else 6.0) * (2.0 ** mfrac)
        th1 = (2.0 ** (mfrac + 1)) ** 2
        th2 = (2.0 ** (mfrac + 2)) ** 2
        src3 = src.rearrange("p (b k) -> p b k", k=32)
        amax, sc_f, sc_b = sa0[:, 0:nb], sa1[:, 0:nb], sa2b[:, 0:nb]
        sc_fb = sa1b[:, 0:nb]
        y, t1, ge = QY(), QT1(), QGE()
        p.op("ve", lambda EG: EG.tensor_reduce(out=amax, in_=src3, axis=AX.X,
                                             op=ALU.max, apply_absolute_value=True))
        p.op("ve", lambda EG: EG.tensor_scalar(out=amax, in0=amax, scalar1=2.0 ** -40,
                                             scalar2=None, op0=ALU.max), sem=sV)
        p.wait("sc", sV)
        p.op("sc", lambda EG: EG.activation(out=sc_f, in_=amax, func=AF.Ln), sem=sA)
        p.wait("ve", sA)
        p.op("ve", lambda EG: EG.tensor_scalar(out=sc_f, in0=sc_f, scalar1=1.0 / LN2,
                                             scalar2=-0.5, op0=ALU.mult, op1=ALU.add),
             sem=sV)
        p.wait("ve", sV)
        p.op("ve", lambda EG: EG.tensor_scalar(out=sc_f, in0=sc_f, scalar1=MAGIC,
                                             scalar2=MAGIC, op0=ALU.add,
                                             op1=ALU.subtract), sem=sV)
        p.wait("sc", sV)
        bcol = 5 if mfrac == 3 else 7
        p.op("sc", lambda EG, bcol=bcol: EG.activation(
            out=sc_b, in_=sc_f, func=AF.Exp,
            scale=flag[:, 2:3], bias=flag[:, bcol:bcol + 1]))
        p.op("sc", lambda EG, bcol=bcol: EG.activation(
            out=sc_fb, in_=sc_f, func=AF.Exp,
            scale=flag[:, 3:4], bias=flag[:, bcol - 1:bcol]), sem=sA)
        p.wait("ve", sA)
        fwd_b = sc_fb.unsqueeze(2).broadcast_to([128, nb, 32])
        bwd_b = sc_b.unsqueeze(2).broadcast_to([128, nb, 32])
        y3 = y.rearrange("p (b k) -> p b k", k=32)
        p.op("ve", lambda EG: EG.tensor_tensor(out=y3, in0=src3, in1=fwd_b, op=ALU.mult))
        p.op("ve", lambda EG: EG.tensor_tensor(out=t1, in0=y, in1=y, op=ALU.mult))
        p.op("ve", lambda EG: EG.tensor_scalar(out=ge, in0=t1, scalar1=th1,
                                             scalar2=None, op0=ALU.is_ge))
        p.op("ve", lambda EG: EG.tensor_tensor(out=t1, in0=ge, in1=y, op=ALU.mult))
        p.op("ve", lambda EG: EG.scalar_tensor_tensor(out=t1, in0=t1, scalar=-0.5,
                                                    in1=y, op0=ALU.mult, op1=ALU.add))
        p.op("ve", lambda EG: EG.tensor_tensor(out=y, in0=y, in1=y, op=ALU.mult))
        p.op("ve", lambda EG: EG.tensor_scalar(out=y, in0=y, scalar1=th2,
                                             scalar2=None, op0=ALU.is_ge))
        p.op("ve", lambda EG: EG.scalar_tensor_tensor(out=ge, in0=y, scalar=2.0,
                                                    in1=ge, op0=ALU.mult, op1=ALU.add))
        p.op("ve", lambda EG: EG.tensor_scalar(out=ge, in0=ge, scalar1=1.0,
                                             scalar2=None, op0=ALU.add))
        p.op("ve", lambda EG: EG.tensor_tensor(out=y, in0=y, in1=t1, op=ALU.mult))
        p.op("ve", lambda EG: EG.scalar_tensor_tensor(out=t1, in0=y, scalar=-0.5,
                                                    in1=t1, op0=ALU.mult, op1=ALU.add))
        p.op("ve", lambda EG: EG.tensor_scalar(out=t1, in0=t1, scalar1=MAGIC,
                                             scalar2=MAGIC, op0=ALU.add,
                                             op1=ALU.subtract))
        p.op("ve", lambda EG: EG.tensor_tensor(out=t1, in0=t1, in1=ge, op=ALU.mult))
        p.op("ve", lambda EG: EG.tensor_scalar(out=t1, in0=t1, scalar1=clip,
                                             scalar2=-clip, op0=ALU.min, op1=ALU.max))
        t13 = t1.rearrange("p (b k) -> p b k", k=32)
        d3 = dst.rearrange("p (b k) -> p b k", k=32)
        p.op("ve", lambda EG: EG.tensor_tensor(out=d3, in0=t13, in1=bwd_b,
                                             op=ALU.mult), sem=sV)

    wf32 = w8ring[:].bitcast(F32)
    SRC32 = lambda: wf32[:, 2048:4096]
    Y32 = lambda: wf32[:, 4096:6144]
    T132 = lambda: wf32[:, 6144:8192]

    def emit_quant32(src32, dst, mfrac):
        nb = 64
        clip = (7.5 if mfrac == 3 else 6.0) * (2.0 ** mfrac)
        th1 = (2.0 ** (mfrac + 1)) ** 2
        th2 = (2.0 ** (mfrac + 2)) ** 2
        src3 = src32.rearrange("p (b k) -> p b k", k=32)
        amax, sc_f, sc_b = sa0[:, 0:nb], sa1[:, 0:nb], sa2b[:, 0:nb]
        sc_fb = sa1b[:, 0:nb]
        y32, t132 = Y32(), T132()
        geA, geB, m1, m2 = QGE(), QY(), QT1(), QSRC()
        p.op("ve", lambda EG: EG.tensor_reduce(out=amax, in_=src3, axis=AX.X,
                                             op=ALU.max, apply_absolute_value=True))
        p.op("ve", lambda EG: EG.tensor_scalar(out=amax, in0=amax, scalar1=2.0 ** -40,
                                             scalar2=None, op0=ALU.max), sem=sV)
        p.wait("sc", sV)
        p.op("sc", lambda EG: EG.activation(out=sc_f, in_=amax, func=AF.Ln), sem=sA)
        p.wait("ve", sA)
        p.op("ve", lambda EG: EG.tensor_scalar(out=sc_f, in0=sc_f, scalar1=1.0 / LN2,
                                             scalar2=-0.5, op0=ALU.mult, op1=ALU.add),
             sem=sV)
        p.wait("ve", sV)
        p.op("ve", lambda EG: EG.tensor_scalar(out=sc_f, in0=sc_f, scalar1=MAGIC,
                                             scalar2=MAGIC, op0=ALU.add,
                                             op1=ALU.subtract), sem=sV)
        p.wait("sc", sV)
        bcol = 5 if mfrac == 3 else 7
        p.op("sc", lambda EG, bcol=bcol: EG.activation(
            out=sc_b, in_=sc_f, func=AF.Exp,
            scale=flag[:, 2:3], bias=flag[:, bcol:bcol + 1]))
        p.op("sc", lambda EG, bcol=bcol: EG.activation(
            out=sc_fb, in_=sc_f, func=AF.Exp,
            scale=flag[:, 3:4], bias=flag[:, bcol - 1:bcol]), sem=sA)
        p.wait("ve", sA)
        fwd_b = sc_fb.unsqueeze(2).broadcast_to([128, nb, 32])
        bwd_b = sc_b.unsqueeze(2).broadcast_to([128, nb, 32])
        y3 = y32.rearrange("p (b k) -> p b k", k=32)
        p.op("ve", lambda EG: EG.tensor_tensor(out=y3, in0=src3, in1=fwd_b,
                                             op=ALU.mult))
        p.op("ve", lambda EG: EG.tensor_tensor(out=t132, in0=y32, in1=y32,
                                             op=ALU.mult))
        p.op("ve", lambda EG: EG.tensor_scalar(out=geA, in0=t132, scalar1=th1,
                                             scalar2=None, op0=ALU.is_ge))
        p.op("ve", lambda EG: EG.tensor_scalar(out=geB, in0=t132, scalar1=th2,
                                             scalar2=None, op0=ALU.is_ge))
        p.op("ve", lambda EG: EG.tensor_scalar(out=m1, in0=geA, scalar1=-0.5,
                                             scalar2=1.0, op0=ALU.mult, op1=ALU.add))
        p.op("ve", lambda EG: EG.tensor_scalar(out=m2, in0=geB, scalar1=-0.5,
                                             scalar2=1.0, op0=ALU.mult, op1=ALU.add))
        p.op("ve", lambda EG: EG.tensor_tensor(out=m1, in0=m1, in1=m2, op=ALU.mult))
        p.op("ve", lambda EG: EG.scalar_tensor_tensor(out=geA, in0=geB, scalar=2.0,
                                                    in1=geA, op0=ALU.mult, op1=ALU.add))
        p.op("ve", lambda EG: EG.tensor_scalar(out=geA, in0=geA, scalar1=1.0,
                                             scalar2=None, op0=ALU.add))
        p.op("ve", lambda EG: EG.tensor_tensor(out=t132, in0=y32, in1=m1,
                                             op=ALU.mult))
        p.op("ve", lambda EG: EG.tensor_scalar(out=t132, in0=t132, scalar1=MAGIC,
                                             scalar2=MAGIC, op0=ALU.add,
                                             op1=ALU.subtract))
        p.op("ve", lambda EG: EG.tensor_tensor(out=t132, in0=t132, in1=geA,
                                             op=ALU.mult))
        p.op("ve", lambda EG: EG.tensor_scalar(out=t132, in0=t132, scalar1=clip,
                                             scalar2=-clip, op0=ALU.min, op1=ALU.max))
        t13 = t132.rearrange("p (b k) -> p b k", k=32)
        d3 = dst.rearrange("p (b k) -> p b k", k=32)
        p.op("ve", lambda EG: EG.tensor_tensor(out=d3, in0=t13, in1=bwd_b,
                                             op=ALU.mult), sem=sV)

    p.op("ve", lambda EG: EG.memset(bigC[:, 0:2048], 0.0), sem=sV)
    p.wait("sc", sV)
    p.op("sc", lambda EG: EG.activation(
        out=bigC[:, 2048:4096], in_=bigC[:, 0:2048], func=AF.Exp), sem=sA)
    p.dma("sy", tril[:], tril_in, sIN)
    p.dma("sy", onesb[:], ones_in, sIN)
    p.dma("sy", flag[:], flag_in, sIN)
    for t in range(T_TILES):
        p.dma("sy", cosT[:, t * HD:(t + 1) * HD], cos_in[t * 128:(t + 1) * 128, :], sIN)
        p.dma("sy", sinT[:, t * HD:(t + 1) * HD], sin_in[t * 128:(t + 1) * 128, :], sIN)
    for t in range(T_TILES):
        for h in range(2):
            p.wait("sy", sV)
            p.dma("sy", tf32[:], x_in[t * 128:(t + 1) * 128,
                                      h * 1024:(h + 1) * 1024], sIN)
            p.wait("ve", sIN)
            p.op("ve", lambda EG, t=t, h=h: EG.tensor_copy(
                out=xres[:, t * E + h * 1024: t * E + (h + 1) * 1024],
                in_=tf32[:]), sem=sV)

    def prep_weights(w_ap, rows, cols, mfrac, dstT, seg_of, out_fp8):
        for r in range(rows // 128):
            for hseg in range(cols // 2048):
                c0 = hseg * 2048
                p.wait("sy", sV)
                p.dma("sy", SRC32(),
                      w_ap[r * 128:(r + 1) * 128, c0:c0 + 2048], sIN)
                p.wait("ve", sIN)
                p.wait("ve", sST, sST.n)
                if mfrac is None:
                    p.op("ve", lambda EG: EG.tensor_copy(out=bigA[:], in_=SRC32()),
                         sem=sV)
                else:
                    emit_quant32(SRC32(), bigA[:], mfrac)
                p.wait("sy", sV)
                p.wait("sy", sST, sST.n)
                for f in range(16):
                    p.dma("sy", bigB[:, f * 128:(f + 1) * 128],
                          bigA[:, f * 128:(f + 1) * 128], sTR, transpose=True)
                p.wait("ve", sTR)
                d_idx, d_off = seg_of(r, hseg)
                if out_fp8:
                    p.op("ve", lambda EG: EG.tensor_copy(
                        out=w8ring[:, 0:2048], in_=bigB[:]), sem=sV)
                    p.wait("po", sV)
                    p.dma("po", dstT[d_idx, :, d_off:d_off + 2048],
                          w8ring[:, 0:2048], sST)
                else:
                    p.wait("po", sV)
                    p.wait("po", sTR)
                    p.dma("po", dstT[d_idx, :, d_off:d_off + 2048], bigB[:], sST)

    prep_weights(wg_in, I_DIM, E, 1, wgT, lambda r, h: (r // 4, (r % 4) * 2048), True)
    prep_weights(wu_in, I_DIM, E, 1, wuT, lambda r, h: (r // 4, (r % 4) * 2048), True)
    prep_weights(wd_in, E, I_DIM, 1, wdT,
                 lambda r, h: (r // 4, (r % 4) * 8192 + h * 2048), True)
    prep_weights(wq_in, 3 * E, E, None, wqT, lambda r, h: (r // 4, (r % 4) * 2048), False)
    prep_weights(wo_in, E, E, None, woT, lambda r, h: (r // 4, (r % 4) * 2048), False)
    barrier()

    def emit_sacrifice(func):
        p.op("sc", lambda EG, func=func: EG.activation(
            out=bigC[:, 2048:4096], in_=xres[:, 0:2048], func=func))

    def emit_ln_stats():
        emit_sacrifice(AF.Square)
        for t in range(T_TILES):
            xt = xres[:, t * E:(t + 1) * E]
            p.op("ve", lambda EG, xt=xt, t=t: EG.reduce_sum(
                out=stats[:, t:t + 1], in_=xt, axis=AX.X))
            p.op("sc", lambda EG, xt=xt, t=t: EG.activation(
                out=bigC[:, 0:E], in_=xt, func=AF.Square,
                accum_out=stats[:, 16 + t:17 + t]), sem=sA)
        emit_sacrifice(AF.Exp)
        p.op("ve", lambda EG: EG.tensor_scalar(
            out=stats[:, 32:48], in0=stats[:, 0:16], scalar1=1.0 / E,
            scalar2=None, op0=ALU.mult), sem=sV)
        p.wait("ve", sA)
        p.op("ve", lambda EG: EG.tensor_scalar(
            out=stats[:, 64:80], in0=stats[:, 16:32], scalar1=1.0 / E,
            scalar2=None, op0=ALU.mult))
        p.wait("ve", sV)
        p.op("ve", lambda EG: EG.tensor_tensor(
            out=stats[:, 80:96], in0=stats[:, 32:48], in1=stats[:, 32:48],
            op=ALU.mult), sem=sV)
        p.wait("ve", sV)
        p.op("ve", lambda EG: EG.tensor_tensor(
            out=stats[:, 64:80], in0=stats[:, 64:80], in1=stats[:, 80:96],
            op=ALU.subtract), sem=sV)
        p.wait("sc", sV)
        p.op("sc", lambda EG: EG.activation(
            out=stats[:, 80:96], in_=stats[:, 64:80], func=AF.Ln,
            bias=flag[:, 8:9]))
        p.op("sc", lambda EG: EG.activation(
            out=stats[:, 48:64], in_=stats[:, 80:96], func=AF.Exp,
            scale=flag[:, 9:10]), sem=sA)
        if _DEBUG:
            p.wait("po", sA, sA.n)
            p.dma("po", dbgA[:, 0:96], stats[:], sST)
            p.wait("ve", sST, sST.n)
        p.wait("ve", sA)
        p.op("ve", lambda EG: EG.tensor_scalar(
            out=stats[:, 0:16], in0=stats[:, 64:80], scalar1=1e-5,
            scalar2=None, op0=ALU.add))
        p.op("ve", lambda EG: EG.tensor_tensor(
            out=stats[:, 80:96], in0=stats[:, 48:64], in1=stats[:, 48:64],
            op=ALU.mult), sem=sV)
        p.wait("ve", sV)
        p.op("ve", lambda EG: EG.tensor_tensor(
            out=stats[:, 80:96], in0=stats[:, 80:96], in1=stats[:, 0:16],
            op=ALU.mult), sem=sV)
        p.wait("ve", sV)
        p.op("ve", lambda EG: EG.tensor_scalar(
            out=stats[:, 80:96], in0=stats[:, 80:96], scalar1=-0.5,
            scalar2=1.5, op0=ALU.mult, op1=ALU.add), sem=sV)
        p.wait("ve", sV)
        p.op("ve", lambda EG: EG.tensor_tensor(
            out=stats[:, 48:64], in0=stats[:, 48:64], in1=stats[:, 80:96],
            op=ALU.mult), sem=sV)
        p.wait("ve", sV)

    def m_b(t, n):
        return stats[:, 32 + t:33 + t].broadcast_to([128, n])

    def rstd_b(t, n):
        return stats[:, 48 + t:49 + t].broadcast_to([128, n])

    def emit_ffn(tag, final):
        emit_ln_stats()
        sXN = p.sem(tag + "xn")
        sB1 = p.sem(tag + "b1")
        aB1 = p.sem(tag + "a1")
        vB1 = p.sem(tag + "v1")
        wC = p.sem(tag + "wc")
        pC = p.sem(tag + "pc")
        stC = p.sem(tag + "st")
        for t in range(T_TILES):
            xt = xres[:, t * E:(t + 1) * E]
            p.op("ve", lambda EG, xt=xt, t=t: EG.tensor_scalar(
                out=QSRC(), in0=xt, scalar1=stats[:, 32 + t:33 + t],
                scalar2=stats[:, 48 + t:49 + t], op0=ALU.subtract, op1=ALU.mult))
            emit_quant(QSRC(), bigA[:], 3)
            if _DEBUG and tag == "f1" and t == 0:
                p.wait("po", sV)
                p.dma("po", dbgA[:, 0:64], sa0[:, 0:64], sST)
                p.dma("po", dbgF[:, 0:64], sa1b[:, 0:64], sST)
                p.dma("po", dbgF[:, 256:320], sa2b[:, 0:64], sST)
                p.dma("po", dbgQ, bigA[:], sST)
                p.dma("po", dbgN, QSRC(), sST)
                p.wait("ve", sST, sST.n)
            p.wait("sy", sV)
            for f in range(ECH):
                p.dma("sy", bigB[:, f * 128:(f + 1) * 128],
                      bigA[:, f * 128:(f + 1) * 128], sTR, transpose=True)
            p.wait("ve", sTR)
            p.op("ve", lambda EG, t=t: EG.tensor_copy(
                out=xnT[:, t * E:(t + 1) * E], in_=bigB[:]), sem=sXN)
        if _DEBUG and tag == "f1":
            p.wait("po", sXN)
            p.dma("po", dbgX, xnT[:], sST)
            p.dma("po", dbgS, stats[:], sST)
        p.wait("pe", sXN)
        for s in range(ISL):
            slot = s % 2
            if s >= 2:
                p.wait("po", wC, s - 1)
            p.dma("po", w8ring[:, slot * 16384: slot * 16384 + 8192], wgT[s], sWQ)
            p.dma("po", w8ring[:, slot * 16384 + 8192: slot * 16384 + 16384],
                  wuT[s], sWQ)
            p.wait("pe", sWQ)
            wview = w8ring[:].rearrange("p (a j f c) -> p a j f c", a=4, j=4, f=16)
            for t in range(T_TILES):
                g = s * T_TILES + t
                bg, bu = psum[2 * (g % 2)], psum[2 * (g % 2) + 1]
                if g >= 2:
                    p.wait("pe", pC, g - 1)
                for f in range(ECH):
                    lhs = xnT[:, t * E + f * 128: t * E + (f + 1) * 128]
                    rg = wview[:, 2 * slot, :, f, :]
                    ru = wview[:, 2 * slot + 1, :, f, :]
                    p.op("pe", lambda EG, lhs=lhs, bg=bg, rg=rg, st=(f == 0),
                         sp=(f == ECH - 1): EG.matmul(bg[:], lhsT=lhs, rhs=rg,
                                                     start=st, stop=sp))
                    p.op("pe", lambda EG, lhs=lhs, bu=bu, ru=ru, st=(f == 0),
                         sp=(f == ECH - 1): EG.matmul(bu[:], lhsT=lhs, rhs=ru,
                                                     start=st, stop=sp),
                         sem=sB1 if f == ECH - 1 else None)
                p.wait("sc", sB1, g + 1)
                sg = bigC[:, (g % 2) * 512:(g % 2) * 512 + 512]
                et = tf32[:, (g % 2) * 512:(g % 2) * 512 + 512]
                p.op("sc", lambda EG, bg=bg, et=et: EG.activation(
                    out=et, in_=bg[:], func=AF.Exp, scale=flag[:, 10:11]), sem=aB1)
                p.wait("ve", aB1, g + 1)
                p.op("ve", lambda EG, et=et: EG.tensor_scalar(
                    out=et, in0=et, scalar1=1.0, scalar2=None, op0=ALU.add))
                p.op("ve", lambda EG, et=et: EG.reciprocal(out=et, in_=et))
                p.op("ve", lambda EG, bg=bg, et=et, sg=sg: EG.tensor_tensor(
                    out=sg, in0=bg[:], in1=et, op=ALU.mult))
                stage = bigB[:, (g % 4) * 512:(g % 4) * 512 + 512]
                if g >= 4:
                    p.wait("ve", stC, 16 * (g - 3))
                p.op("ve", lambda EG, sg=sg, bu=bu, stage=stage: EG.tensor_tensor(
                    out=stage, in0=sg, in1=bu[:], op=ALU.mult), sem=vB1)
                p.tick("ve", pC)
                p.wait("po", vB1, g + 1)
                p.dma("po", interD[t, :, s * 512:(s + 1) * 512], stage, stC)
            p.tick("pe", wC)
        wC2 = p.sem(tag + "w2")
        iqR = p.sem(tag + "ir")
        iqF = p.sem(tag + "if")
        sB2 = p.sem(tag + "b2")
        pC2 = p.sem(tag + "p2")
        outC = p.sem(tag + "oc")
        for hs in range(HSL):
            if hs >= 1:
                p.wait("po", wC2, hs)
            p.wait("po", stC)
            p.dma("po", w8ring[:], wdT[hs], sWQ)
            p.wait("pe", sWQ)
            dview = w8ring[:].rearrange("p (j k c) -> p j k c", j=4, k=64)
            for t in range(T_TILES):
                gi = hs * T_TILES + t
                slot = gi % 2
                iqs = xnT[:, slot * 8192:(slot + 1) * 8192]
                if hs == 0:
                    if gi >= 2:
                        p.wait("ve", iqF, gi - 1)
                    for c in range(4):
                        p.wait("po", sV, sV.n)
                        p.dma("po", QSRC(), interD[t, :, c * 2048:(c + 1) * 2048], sWQ)
                        p.wait("ve", sWQ)
                        emit_quant(QSRC(), bigA[:], 3)
                        p.wait("sy", sV)
                        for k in range(16):
                            p.dma("sy", bigB[:, k * 128:(k + 1) * 128],
                                  bigA[:, k * 128:(k + 1) * 128], sTR, transpose=True)
                        p.wait("ve", sTR)
                        p.op("ve", lambda EG, iqs=iqs, c=c: EG.tensor_copy(
                            out=iqs[:, c * 2048:(c + 1) * 2048], in_=bigB[:]),
                            sem=iqR if c == 3 else None)
                    p.wait("po", iqR)
                    p.dma("po", iqTD[t], iqs, sST)
                else:
                    if gi >= 2:
                        p.wait("po", iqF, gi - 1)
                    p.dma("po", iqs, iqTD[t], sWQ)
                    p.wait("po", sWQ)
                    p.tick("po", iqR)
                p.wait("pe", iqR, gi + 1)
                bd = psum[4 + (gi % 2)]
                if gi >= 2:
                    p.wait("pe", pC2, gi - 1)
                for k in range(ICH):
                    rd = dview[:, :, k, :]
                    p.op("pe", lambda EG, bd=bd, iqs=iqs, rd=rd, k=k, st=(k == 0),
                         sp=(k == ICH - 1): EG.matmul(
                             bd[:], lhsT=iqs[:, k * 128:(k + 1) * 128],
                             rhs=rd, start=st, stop=sp),
                         sem=sB2 if k == ICH - 1 else None)
                p.tick("pe", iqF)
                p.wait("ve", sB2, gi + 1)
                xslice = xres[:, t * E + hs * 512: t * E + hs * 512 + 512]
                if final:
                    stage2 = tf32[:, (gi % 2) * 512:(gi % 2) * 512 + 512]
                    if gi >= 2:
                        p.wait("ve", outC, 16 * (gi - 1))
                    p.op("ve", lambda EG, bd=bd, xs=xslice, st2=stage2:
                         EG.scalar_tensor_tensor(out=st2, in0=bd[:], scalar=0.5,
                                                in1=xs, op0=ALU.mult, op1=ALU.add),
                         sem=vB1)
                    p.tick("ve", pC2)
                    p.wait("po", vB1)
                    p.dma("po", y_out[t * 128:(t + 1) * 128,
                                      hs * 512:(hs + 1) * 512], stage2, outC)
                else:
                    p.op("ve", lambda EG, bd=bd, xs=xslice:
                         EG.scalar_tensor_tensor(out=xs, in0=bd[:], scalar=0.5,
                                                in1=xs, op0=ALU.mult, op1=ALU.add),
                         sem=pC2)
            p.tick("pe", wC2)
        barrier()

    def emit_attn():
        emit_ln_stats()
        if _DEBUG:
            p.wait("po", sV, sV.n)
            p.dma("po", dbgS, stats[:], sST)
            p.wait("ve", sST, sST.n)
        wCa = p.sem("awc")
        sQK = p.sem("aqk")
        pCa = p.sem("apc")
        vQK = p.sem("avq")
        stQ = p.sem("asq")
        for tb in range(4):
            p.wait("sy", sQK, sQK.n)
            for ti in range(4):
                t = tb * 4 + ti
                xt = xres[:, t * E:(t + 1) * E]
                p.wait("ve", sTR, sTR.n)
                p.op("ve", lambda EG, xt=xt, t=t: EG.tensor_scalar(
                    out=bigA[:], in0=xt, scalar1=stats[:, 32 + t:33 + t],
                    scalar2=stats[:, 48 + t:49 + t], op0=ALU.subtract,
                    op1=ALU.mult), sem=sV)
                p.wait("sy", sV)
                for f in range(ECH):
                    p.dma("sy", bigC[:, ti * E + f * 128: ti * E + (f + 1) * 128],
                          bigA[:, f * 128:(f + 1) * 128], sTR, transpose=True)
            for osl in range(QKV_SL):
                gq = tb * QKV_SL + osl
                slot = gq % 2
                if gq >= 2:
                    p.wait("po", wCa, gq - 1)
                p.dma("po", wbf[:, slot * 8192:(slot + 1) * 8192], wqT[osl], sWQ)
                p.wait("pe", sWQ)
                p.wait("pe", sTR)
                qview = wbf[:].rearrange("p (s j f c) -> p s j f c", s=2, j=4, f=16)
                for ti in range(4):
                    t = tb * 4 + ti
                    g2 = gq * 4 + ti
                    bq = psum[g2 % 2]
                    if g2 >= 2:
                        p.wait("pe", pCa, g2 - 1)
                    for f in range(ECH):
                        lhs = bigC[:, ti * E + f * 128: ti * E + (f + 1) * 128]
                        rq = qview[:, slot, :, f, :]
                        p.op("pe", lambda EG, lhs=lhs, bq=bq, rq=rq, st=(f == 0),
                             sp=(f == ECH - 1): EG.matmul(bq[:], lhsT=lhs, rhs=rq,
                                                         start=st, stop=sp),
                             sem=sQK if f == ECH - 1 else None)
                    p.wait("ve", sQK, sQK.n)
                    typ = osl // 4
                    stage = bigB[:, (g2 % 4) * 512:(g2 % 4) * 512 + 512]
                    if g2 >= 4:
                        p.wait("ve", stQ, 16 * (g2 - 3))
                    if typ < 2:
                        cos4 = cosT[:, t * HD:(t + 1) * HD].unsqueeze(1).broadcast_to(
                            [128, 4, HD])
                        sm4 = sinT[:, t * HD:(t + 1) * HD].unsqueeze(1).broadcast_to(
                            [128, 4, HD]).rearrange("p h (q two) -> p h q two", two=2)
                        ps3 = bq[:].rearrange("p (h d) -> p h d", d=HD)
                        e2 = tf32[:, 0:512]
                        tm = tf32[:, 512:1024]
                        e3 = e2.rearrange("p (h d) -> p h d", d=HD)
                        tm4 = tm.rearrange("p (h q two) -> p h q two", two=2, q=64)
                        ps_sw = bq[:].rearrange("p (h q two) -> p h q two",
                                                two=2, q=64)[:, :, :, ::-1]
                        p.op("ve", lambda EG, ps3=ps3, cos4=cos4, e3=e3:
                             EG.tensor_tensor(out=e3, in0=ps3, in1=cos4, op=ALU.mult))
                        p.op("ve", lambda EG, ps_sw=ps_sw, sm4=sm4, tm4=tm4:
                             EG.tensor_tensor(out=tm4, in0=ps_sw, in1=sm4, op=ALU.mult))
                        p.op("ve", lambda EG, e2=e2, tm=tm:
                             EG.tensor_tensor(out=e2, in0=e2, in1=tm, op=ALU.add))
                        p.op("ve", lambda EG, e2=e2, tm=tm: EG.tensor_scalar(
                            out=tm, in0=e2, scalar1=0.0, scalar2=None, op0=ALU.max))
                        p.op("ve", lambda EG, e2=e2: EG.tensor_scalar(
                            out=e2, in0=e2, scalar1=0.0, scalar2=None,
                            op0=ALU.min), sem=sV)
                        p.wait("sc", sV)
                        p.op("sc", lambda EG, e2=e2, stage=stage: EG.activation(
                            out=stage, in_=e2, func=AF.Exp), sem=sA)
                        p.wait("ve", sA)
                        p.op("ve", lambda EG, stage=stage, tm=tm: EG.tensor_tensor(
                            out=stage, in0=stage, in1=tm, op=ALU.add), sem=vQK)
                    else:
                        p.op("ve", lambda EG, bq=bq, stage=stage: EG.tensor_copy(
                            out=stage, in_=bq[:]), sem=vQK)
                    p.tick("ve", pCa)
                    p.wait("po", vQK)
                    dstD = (qD, kD, vD)[typ]
                    col = (osl % 4) * 512
                    p.dma("po", dstD[t * 128:(t + 1) * 128, col:col + 512], stage, stQ)
                p.tick("pe", wCa)
        barrier()

        qT = xv[:, 0:4096]
        kT = xv[:, 4096:8192]
        kt = xv[:, 8192:12288]
        vp = bigC[:, 0:4128]
        Sbf = bigC[:, 4128:6192]
        attw = bigC[:, 6192:6704]
        aout = wbf[:, 0:4096]
        aoT = wbf[:, 4096:8192]
        wout = wbf[:, 8192:16384]
        vpv = vp.rearrange("p (a h q) -> p a h q", a=2, h=NH)

        sS = p.sem("ass")
        vS = p.sem("avs")
        sL = p.sem("asl")
        sT2 = p.sem("at2")
        nC = p.sem("anc")
        aC = p.sem("aac")
        oC = p.sem("aoc")

        def load_chunk(c, with_qk):
            for mt in range(2):
                p.dma("po", kt[:, mt * 2048:(mt + 1) * 2048],
                      kD[c * 256 + mt * 128: c * 256 + (mt + 1) * 128, :], sL)
                p.dma("po", vpv[:, mt, :, 0:128],
                      vD[c * 256 + mt * 128: c * 256 + (mt + 1) * 128, :]
                      .rearrange("p (h d) -> p h d", h=NH), sL)
            if with_qk:
                for hh2 in range(NH):
                    p.dma("sy", qT[:, hh2 * 256:(hh2 + 1) * 256],
                          qD[c * 256:(c + 1) * 256, hh2 * 128:(hh2 + 1) * 128],
                          sT2, transpose=True)
                    p.dma("sy", kT[:, hh2 * 256:(hh2 + 1) * 256],
                          kD[c * 256:(c + 1) * 256, hh2 * 128:(hh2 + 1) * 128],
                          sT2, transpose=True)
            p.wait("ve", sL)
            ones_bc = onesb[:, 0:1].unsqueeze(2).broadcast_to([128, 2, NH, 1])
            vpo = vpv[:, :, :, 128:129]
            p.op("ve", lambda EG: EG.tensor_copy(out=vpo, in_=ones_bc), sem=vS)

        def state_update(c):
            for hh in range(NH):
                bs = psum[6 + (hh % 2)]
                p.wait("pe", vS, vS.n)
                for mt in range(2):
                    lhs = kt[:, mt * 2048 + hh * 128: mt * 2048 + (hh + 1) * 128]
                    rhs = vp.rearrange("p (a q) -> p a q", a=2)[:, mt,
                                                               hh * 129:(hh + 1) * 129]
                    p.op("pe", lambda EG, bs=bs, lhs=lhs, rhs=rhs, st=(mt == 0),
                         sp=(mt == 1): EG.matmul(bs[:, 0:129], lhsT=lhs, rhs=rhs,
                                                start=st, stop=sp),
                         sem=sS if mt == 1 else None)
                p.wait("ve", sS)
                p.op("ve", lambda EG, bs=bs, hh=hh: EG.tensor_tensor(
                    out=Sfp[:, hh * 129:(hh + 1) * 129],
                    in0=Sfp[:, hh * 129:(hh + 1) * 129],
                    in1=bs[:, 0:129], op=ALU.add), sem=vS)

        p.op("ve", lambda EG: EG.memset(Sfp[:], 0.0), sem=sV)
        p.wait("ve", sV)
        for c in range(NCHUNK):
            p.wait("po", vS, vS.n)
            p.wait("po", sS, sS.n)
            load_chunk(c, with_qk=False)
            p.wait("pe", vS)
            p.wait("pe", sL)
            state_update(c)
        p.wait("ve", sS)
        fb = flag[:, 1:2].broadcast_to([128, NH * 129])
        p.op("ve", lambda EG: EG.tensor_tensor(out=Sfp[:], in0=Sfp[:], in1=fb,
                                             op=ALU.mult), sem=sV)
        p.wait("po", sV)
        p.dma("po", ccS, Sfp[:], sST)
        p.wait("po", sST)
        p.op("po", lambda EG: EG.collective_compute(
            "AllReduce", ALU.add,
            replica_groups=[[0, 1], [2, 3], [4, 5], [6, 7]],
            ins=[ccS.opt()], outs=[ccD.opt()]), sem=sCC)
        p.wait("po", sCC)
        if _DEBUG:
            p.dma("po", ccE, ccD, sST)
        p.dma("po", Sfp[:], ccD, sWQ)
        p.wait("ve", sWQ)
        fa = flag[:, 0:1].broadcast_to([128, NH * 129])
        p.op("ve", lambda EG: EG.tensor_tensor(out=Sfp[:], in0=Sfp[:], in1=fa,
                                             op=ALU.mult))
        p.op("ve", lambda EG: EG.tensor_copy(out=Sbf, in_=Sfp[:]), sem=vS)

        for c in range(NCHUNK):
            p.wait("po", vS, vS.n)
            p.wait("po", sS, sS.n)
            p.wait("po", nC, nC.n)
            load_chunk(c, with_qk=True)
            p.wait("ve", sT2, sT2.n)
            p.wait("pe", vS)
            p.wait("pe", sL)
            p.wait("pe", sT2)
            for hh in range(NH):
                pa0, pa1 = psum[(hh % 2) * 2], psum[(hh % 2) * 2 + 1]
                pn0, pn1 = psum[4], psum[5]
                if hh >= 2:
                    p.wait("pe", aC, 2 * (hh - 1))
                for mt, pa in ((0, pa0), (1, pa1)):
                    lhs = kT[:, hh * 256 + mt * 128: hh * 256 + (mt + 1) * 128]
                    p.op("pe", lambda EG, pa=pa, lhs=lhs, hh=hh: EG.matmul(
                        pa[:, 0:256], lhsT=lhs,
                        rhs=qT[:, hh * 256:(hh + 1) * 256],
                        start=True, stop=True), sem=sS)
                p.wait("ve", sS)
                for mt, pa in ((0, pa0), (1, pa1)):
                    p.op("ve", lambda EG, pa=pa, mt=mt: EG.tensor_tensor(
                        out=attw[:, mt * 256:(mt + 1) * 256], in0=pa[:, 0:256],
                        in1=tril[:, mt * 256:(mt + 1) * 256], op=ALU.mult),
                        sem=aC)
                if hh >= 1:
                    p.wait("pe", nC, 2 * hh)
                p.wait("pe", aC)
                for tt, pn in ((0, pn0), (1, pn1)):
                    for mt in range(2):
                        lhs = attw[:, mt * 256 + tt * 128: mt * 256 + (tt + 1) * 128]
                        rhs = vp.rearrange("p (a q) -> p a q", a=2)[
                            :, mt, hh * 129:(hh + 1) * 129]
                        p.op("pe", lambda EG, pn=pn, lhs=lhs, rhs=rhs, st=(mt == 0):
                             EG.matmul(pn[:, 0:129], lhsT=lhs, rhs=rhs,
                                      start=st, stop=False))
                    lhs = qT[:, hh * 256 + tt * 128: hh * 256 + (tt + 1) * 128]
                    p.op("pe", lambda EG, pn=pn, lhs=lhs, hh=hh: EG.matmul(
                        pn[:, 0:129], lhsT=lhs, rhs=Sbf[:, hh * 129:(hh + 1) * 129],
                        start=False, stop=True), sem=sS)
                bs = psum[6 + (hh % 2)]
                if hh >= 2:
                    p.wait("pe", vS, vS.n)
                for mt in range(2):
                    lhs = kt[:, mt * 2048 + hh * 128: mt * 2048 + (hh + 1) * 128]
                    rhs = vp.rearrange("p (a q) -> p a q", a=2)[
                        :, mt, hh * 129:(hh + 1) * 129]
                    p.op("pe", lambda EG, bs=bs, lhs=lhs, rhs=rhs, st=(mt == 0),
                         sp=(mt == 1): EG.matmul(bs[:, 0:129], lhsT=lhs, rhs=rhs,
                                                start=st, stop=sp),
                         sem=sS if mt == 1 else None)
                p.wait("ve", sS)
                dn2 = tf32[:, 900:902]
                for tt, pn in ((0, pn0), (1, pn1)):
                    p.op("ve", lambda EG, pn=pn, tt=tt: EG.tensor_scalar(
                        out=tf32[:, 900 + tt:901 + tt], in0=pn[:, 128:129],
                        scalar1=1e-6, scalar2=None, op0=ALU.max), sem=sV)
                p.wait("ve", sV)
                p.op("ve", lambda EG, dn2=dn2: EG.reciprocal(out=dn2, in_=dn2),
                     sem=sV)
                p.wait("ve", sV)
                for tt, pn in ((0, pn0), (1, pn1)):
                    dn = tf32[:, 900 + tt:901 + tt]
                    p.op("ve", lambda EG, pn=pn, dn=dn, tt=tt, hh=hh: EG.tensor_tensor(
                        out=aout[:, tt * 2048 + hh * 128: tt * 2048 + (hh + 1) * 128],
                        in0=pn[:, 0:128], in1=dn.broadcast_to([128, 128]),
                        op=ALU.mult), sem=nC)
                p.op("ve", lambda EG, bs=bs, hh=hh: EG.tensor_tensor(
                    out=Sfp[:, hh * 129:(hh + 1) * 129],
                    in0=Sfp[:, hh * 129:(hh + 1) * 129],
                    in1=bs[:, 0:129], op=ALU.add), sem=sV)
                p.wait("ve", sV)
                p.op("ve", lambda EG, hh=hh: EG.tensor_copy(
                    out=Sbf[:, hh * 129:(hh + 1) * 129],
                    in_=Sfp[:, hh * 129:(hh + 1) * 129]), sem=vS)
            p.wait("sy", nC)
            for tt in range(2):
                for hd in range(16):
                    p.dma("sy", aoT[:, hd * 256 + tt * 128: hd * 256 + (tt + 1) * 128],
                          aout[:, tt * 2048 + hd * 128: tt * 2048 + (hd + 1) * 128],
                          sT2, transpose=True)
            for osl2 in range(4):
                p.wait("po", oC, oC.n)
                p.dma("po", wout, woT[osl2], sWQ)
                p.wait("pe", sWQ)
                p.wait("pe", sT2)
                oview = wout.rearrange("p (j f c) -> p j f c", j=4, f=16)
                for tt in range(2):
                    go = osl2 * 2 + tt
                    bo = psum[4 + (go % 2)]
                    if go >= 2:
                        p.wait("pe", oC, go - 1)
                    for hd in range(16):
                        lhs = aoT[:, hd * 256 + tt * 128: hd * 256 + (tt + 1) * 128]
                        ro = oview[:, :, hd, :]
                        p.op("pe", lambda EG, bo=bo, lhs=lhs, ro=ro, st=(hd == 0),
                             sp=(hd == 15): EG.matmul(bo[:], lhsT=lhs, rhs=ro,
                                                     start=st, stop=sp),
                             sem=sS if hd == 15 else None)
                    p.wait("ve", sS)
                    tslice = xres[:, (c * 2 + tt) * E + osl2 * 512:
                                  (c * 2 + tt) * E + osl2 * 512 + 512]
                    p.op("ve", lambda EG, bo=bo, ts=tslice: EG.tensor_tensor(
                        out=ts, in0=ts, in1=bo[:], op=ALU.add), sem=oC)
            p.wait("pe", oC)
        barrier()

    fy = p.sem("fy")

    def emit_flush_y():
        for t in range(T_TILES):
            for hh in range(2):
                p.wait("ve", fy, fy.n)
                p.op("ve", lambda EG, t=t, hh=hh: EG.tensor_copy(
                    out=tf32[:], in_=xres[:, t * E + hh * 1024: t * E + (hh + 1) * 1024]),
                    sem=sV)
                p.wait("po", sV)
                p.dma("po", y_out[t * 128:(t + 1) * 128, hh * 1024:(hh + 1) * 1024],
                      tf32[:], fy)
        p.wait("po", fy)

    final_is_ffn = _PHASES and _PHASES[-1] in ("f1", "f3")
    for ph in _PHASES:
        if ph == "at":
            emit_attn()
        else:
            emit_ffn(ph, final=(ph == _PHASES[-1]))
    if not final_is_ffn:
        emit_flush_y()
    else:
        p.wait("po", p.sems[_PHASES[-1] + "oc"])
        p.wait("sy", p.sems[_PHASES[-1] + "oc"])

    block = st.enter_context(nc.Block())
    p.finalize(block)
    st.close()
    return nc


_CACHE = {}


def kernel(L0_attn_x, L0_attn_W_qkv, L0_attn_W_out, L0_ffn_w_gate,
           L0_ffn_w_up, L0_ffn_w_down):
    x = np.ascontiguousarray(np.asarray(L0_attn_x, np.float32))
    wq = np.ascontiguousarray(np.asarray(L0_attn_W_qkv, np.float32))
    wo = np.ascontiguousarray(np.asarray(L0_attn_W_out, np.float32))
    wg = np.ascontiguousarray(np.asarray(L0_ffn_w_gate, np.float32))
    wu = np.ascontiguousarray(np.asarray(L0_ffn_w_up, np.float32))
    wd = np.ascontiguousarray(np.asarray(L0_ffn_w_down, np.float32))

    if "nc" not in _CACHE:
        _CACHE["nc"] = build_program()
    nc = _CACHE["nc"]

    trilm = host_tril()
    ones = np.ones((128, 1), dtype=ml_dtypes.bfloat16)
    tabs = [host_tables(0), host_tables(1)]
    in_maps = []
    for c in range(8):
        b, h = c // 2, c % 2
        fl = np.zeros((128, 12), np.float32)
        fl[:, 0] = float(h)
        fl[:, 1] = 1.0 - float(h)
        fl[:, 2] = LN2
        fl[:, 3] = -LN2
        fl[:, 4] = 5.0 * LN2
        fl[:, 5] = -5.0 * LN2
        fl[:, 6] = 3.0 * LN2
        fl[:, 7] = -3.0 * LN2
        fl[:, 8] = 1e-5; fl[:, 9] = -0.5; fl[:, 10] = -1.0
        in_maps.append({
            "x": np.ascontiguousarray(x[b, h * 2048:(h + 1) * 2048, :]),
            "w_gate": wg, "w_up": wu, "w_down": wd, "w_qkv": wq, "w_out": wo,
            "cos_t": tabs[h][0], "sin_t": tabs[h][1],
            "tril_t": trilm, "ones_t": ones, "flag_t": fl,
        })
    res = run_bass_kernel_spmd(nc, in_maps, core_ids=list(range(8)))
    y = np.empty((4, 4096, 2048), np.float32)
    for c in range(8):
        b, h = c // 2, c % 2
        y[b, h * 2048:(h + 1) * 2048, :] = res.results[c]["y"]
    return y
